# version 01
# baseline (speedup 1.0000x reference)
import numpy as np
import ml_dtypes
import concourse.bass as bass
import concourse.mybir as mybir
from concourse.bass_utils import run_bass_kernel_spmd

F32 = mybir.dt.float32
BF16 = mybir.dt.bfloat16
AF = mybir.ActivationFunctionType
ALU = mybir.AluOpType
AX = mybir.AxisListType
BF = ml_dtypes.bfloat16

D = 1024
S = 4096
NEG = -1.0e9
EPS = 1e-5

ENGS = ("pe", "dve", "act", "pool", "sp")
DMASEM = {"sp": "dsp", "act": "dact", "pool": "dpool"}
ALLSEMS = list(ENGS) + ["dsp", "dact", "dpool"]


def key_of(x):
    if x is None:
        return None
    if isinstance(x, (str, tuple)):
        return x
    t = getattr(x, "tensor", None)
    if t is not None:
        return t.name
    return x.name


class Prog:

    def __init__(self, nc):
        self.nc = nc
        self.q = {e: [] for e in ENGS}
        self.cnt = {s: 0 for s in ALLSEMS}
        self.waited = {e: {s: 0 for s in ALLSEMS} for e in ENGS}
        self.writer = {}
        self.readers = {}
        self.nops = 0
        self.guard = None

    def _wait(self, eng, sem, val, force_self=False):
        if val <= 0:
            return
        if sem == eng and not force_self:
            return
        if self.waited[eng][sem] >= val:
            return
        self.waited[eng][sem] = val
        self.q[eng].append(("w", sem, val))

    def op(self, eng, fn, reads=(), writes=(), dma=False, scalar_reads=()):
        for a in scalar_reads:
            k = key_of(a)
            w = self.writer.get(k)
            if w and w[0] == eng and w[1] > self.cnt[eng] - 12:
                self._wait(eng, eng, w[1], force_self=True)
        rk = [key_of(a) for a in reads if a is not None]
        wk = [key_of(a) for a in writes if a is not None]
        def dep_val(sem, c):
            return self.cnt[sem] if sem in ("dsp", "dact", "dpool") else c
        for k in rk:
            w = self.writer.get(k)
            if w:
                self._wait(eng, w[0], dep_val(*w), force_self=dma)
        for k in wk:
            w = self.writer.get(k)
            if w:
                self._wait(eng, w[0], dep_val(*w), force_self=dma)
            for (s, c) in self.readers.get(k, ()):
                self._wait(eng, s, dep_val(s, c), force_self=dma)
        sem = DMASEM[eng] if dma else eng
        n = 16 if dma else 1
        self.cnt[sem] += n
        done = (sem, self.cnt[sem])
        self.q[eng].append(("o", fn, sem, n))
        self.nops += 1
        for k in rk:
            self.readers.setdefault(k, []).append(done)
        for k in wk:
            self.writer[k] = done
            self.readers[k] = []
        return done

    def mm(self, out, lhsT, rhs, start=True, stop=True):
        self.op("pe", lambda e, o=out, l=lhsT, r=rhs, s1=start, s2=stop:
                e.matmul(o, l, r, start=s1, stop=s2),
                reads=[lhsT, rhs], writes=[out])

    def transp(self, out_ps, in_sb, ident, tile_position=None):
        self.op("pe", lambda e, o=out_ps, i=in_sb, d=ident, tp=tile_position:
                e.transpose(o, i, d, tile_position=tp),
                reads=[in_sb, ident], writes=[out_ps])

    def cp(self, eng, out, in_):
        if eng == "act":
            self.op(eng, lambda e, o=out, i=in_: e.copy(o, i), reads=[in_], writes=[out])
        else:
            self.op(eng, lambda e, o=out, i=in_: e.tensor_copy(o, i), reads=[in_], writes=[out])

    def act(self, out, in_, func, bias=0.0, scale=1.0, accum=None):
        extra = [b for b in (bias, scale) if not isinstance(b, float)]
        wr = [out] + ([accum] if accum is not None else [])
        self.op("act", lambda e, o=out, i=in_, f=func, b=bias, s=scale, a=accum:
                e.activation(o, i, f, bias=b, scale=s, accum_out=a),
                reads=[in_] + extra, writes=wr, scalar_reads=extra)

    def tt(self, eng, out, in0, in1, alu):
        self.op(eng, lambda e, o=out, a=in0, b=in1, p=alu: e.tensor_tensor(o, a, b, p),
                reads=[in0, in1], writes=[out])

    def ts(self, eng, out, in0, s1, s2, op0, op1=None):
        extra = [s for s in (s1, s2) if s is not None and not isinstance(s, float)]
        self.op(eng, lambda e, o=out, i=in0, a=s1, b=s2, p0=op0, p1=op1:
                e.tensor_scalar(o, i, a, b, p0, p1) if p1 is not None
                else e.tensor_scalar(o, i, a, b, p0),
                reads=[in0] + extra, writes=[out], scalar_reads=extra)

    def stt(self, eng, out, in0, scalar, in1, op0, op1):
        extra = [scalar] if not isinstance(scalar, float) else []
        self.op(eng, lambda e, o=out, i0=in0, s=scalar, i1=in1, p0=op0, p1=op1:
                e.scalar_tensor_tensor(o, i0, s, i1, p0, p1),
                reads=[in0, in1] + extra, writes=[out], scalar_reads=extra)

    def red(self, out, in_, alu=None, is_max=True, negate=False):
        if is_max:
            self.op("dve", lambda e, o=out, i=in_, n=negate:
                    e.reduce_max(o, i, axis=AX.X, negate=n), reads=[in_], writes=[out])
        else:
            self.op("dve", lambda e, o=out, i=in_, n=negate:
                    e.reduce_sum(o, i, axis=AX.X, negate=n), reads=[in_], writes=[out])

    def recip(self, out, in_):
        self.op("dve", lambda e, o=out, i=in_: e.reciprocal(o, i), reads=[in_],
                writes=[out], scalar_reads=[in_])
        self._wait("dve", "dve", self.cnt["dve"], force_self=True)

    def vmax8(self, out, in_):
        self.op("dve", lambda e, o=out, i=in_: e.max(out=o, in_=i), reads=[in_],
                writes=[out], scalar_reads=[in_])
        self._wait("dve", "dve", self.cnt["dve"], force_self=True)

    def memset(self, eng, ap, val):
        self.op(eng, lambda e, a=ap, v=val: e.memset(a, v), reads=[], writes=[ap])

    def dma(self, queue, out, in_, rkeys=(), wkeys=(), accum=False):
        if accum:
            self.op(queue, lambda e, o=out, i=in_: e.dma_start(out=o, in_=i, accum_op=ALU.add),
                    reads=[in_, out] + list(rkeys), writes=[out] + list(wkeys), dma=True)
        else:
            self.op(queue, lambda e, o=out, i=in_: e.dma_start(out=o, in_=i),
                    reads=[in_] + list(rkeys), writes=[out] + list(wkeys), dma=True)

    def barrier(self):
        snap = dict(self.cnt)
        for e in ENGS:
            for s in ALLSEMS:
                if s == e:
                    continue
                self._wait(e, s, snap[s])
        self.writer = {}
        self.readers = {}

    def finalize(self):
        for s in ("dsp", "dact", "dpool"):
            self._wait("sp", s, self.cnt[s])

    def build(self, extra_ctx=None):
        import contextlib
        nc = self.nc
        stack = contextlib.ExitStack()
        sems = {s: stack.enter_context(nc.semaphore(f"sem_{s}")) for s in ALLSEMS}
        with stack:
            with nc.Block() as blk:
                for ename, attr in (("pe", "tensor"), ("dve", "vector"),
                                    ("act", "scalar"), ("pool", "gpsimd"),
                                    ("sp", "sync")):
                    ops = self.q[ename]

                    def body(e, ops=ops, sems=sems):
                        for item in ops:
                            if item[0] == "w":
                                e.wait_ge(sems[item[1]], item[2])
                            else:
                                ins = item[1](e)
                                ins.then_inc(sems[item[2]], item[3])
                    getattr(blk, attr)(body)



def build_l0_masks():
    r = np.arange(128)[:, None]
    j = np.arange(192)[None, :]
    band = (j >= r + 1) & (j <= r + 64)
    mloc = np.where(band, 0.0, NEG).astype(np.float32)
    mloc0 = np.where(band & (j >= 64), 0.0, NEG).astype(np.float32)
    n = np.arange(64)[None, :]
    manch = np.zeros((128, 32, 64), np.float32)
    for c in range(32):
        ok = (64 * n + 63) <= (128 * c + r - 64)
        manch[:, c, :] = np.where(ok, 0.0, NEG)
    return mloc, mloc0, manch.reshape(128, 32 * 64)



def emit_ln_rows(P, sx_row, sxx_row, rows, ident_tag=""):
    m, r, t1, t2 = rows
    P.ts("dve", m, sx_row, 1.0 / D, None, ALU.mult)
    P.ts("dve", t1, sxx_row, 1.0 / D, None, ALU.mult)
    P.tt("dve", t2, m, m, ALU.mult)
    P.tt("dve", t1, t1, t2, ALU.subtract)
    P.act(t2, t1, AF.Sqrt, bias=EPS)
    P.recip(r, t2)


def emit_stats(P, bf_tiles, sq_scr, ones_bf, ps_sx, ps_sxx, square_from=None):
    nt = len(bf_tiles)
    for i, tl in enumerate(bf_tiles):
        P.act(sq_scr, tl, AF.Square)
        P.mm(ps_sx, ones_bf, tl, start=(i == 0), stop=(i == nt - 1))
        P.mm(ps_sxx, ones_bf, sq_scr, start=(i == 0), stop=(i == nt - 1))


def emit_ln_store(P, io, yx, colslice, lnbf, sqbf, rows, obf, d_f32, d_bf, CH):
    bank = io["banks"]
    ones_bf = io["ones_bf"]
    ones_row = io["ones_row"]
    for i in range(8):
        P.cp("dve", lnbf[:, i, :], yx[:, i, :])
    emit_stats(P, [lnbf[:, i, :] for i in range(8)], sqbf[:], ones_bf,
               bank[2][0:1, 0:CH], bank[3][0:1, 0:CH])
    emit_ln_rows(P, bank[2][0:1, 0:CH], bank[3][0:1, 0:CH], [r[:] for r in rows])
    P.mm(bank[4][:, 0:CH], ones_row, rows[0][0:1, 0:CH])
    P.mm(bank[5][:, 0:CH], ones_row, rows[1][0:1, 0:CH])
    for i in range(8):
        P.tt("dve", yx[:, i, :], yx[:, i, :], bank[4][:, 0:CH], ALU.subtract)
        P.tt("dve", yx[:, i, :], yx[:, i, :], bank[5][:, 0:CH], ALU.mult)
        row0 = 128 * i
        P.dma("act", d_f32[row0:row0 + 128, colslice], yx[:, i, :])
        if d_bf is not None:
            P.cp("dve", obf[:], yx[:, i, :])
            P.dma("act", d_bf[row0:row0 + 128, colslice], obf[:])


def emit_gates(P, io, lg_ps, gsc):
    lg = gsc["lg"]
    mx = gsc["mx"]
    gc = gsc["gc"]
    e1 = gsc["e1"]
    e2 = gsc["e2"]
    P.cp("dve", lg[:], lg_ps)
    P.vmax8(mx[:], lg[:])
    P.tt("dve", gc[:, 0:1], mx[:, 0:1], mx[:, 1:2], ALU.subtract)
    P.act(gc[:, 1:2], gc[:, 0:1], AF.Sigmoid)
    P.act(gc[:, 2:3], gc[:, 0:1], AF.Sigmoid, scale=-1.0)
    P.ts("dve", e1[:], lg[:], mx[:, 0:1], None, ALU.is_equal)
    P.ts("dve", e2[:], lg[:], mx[:, 1:2], None, ALU.is_equal)
    P.act(e1[:], e1[:], AF.Identity, scale=gc[:, 1:2])
    P.act(e2[:], e2[:], AF.Identity, scale=gc[:, 2:3])
    P.tt("dve", gsc["gates_bf"][:], e1[:], e2[:], ALU.add)



def emit_moe(P, io, variant):
    nc = P.nc
    bank = io["banks"]
    ident = io["ident"]
    ones_bf = io["ones_bf"]
    ones_row = io["ones_row"]
    ones_row_bf = io["ones_row_bf"]
    s_f32, s_bf = io["s_f32"], io["s_bf"]
    acc_d = io["acc_dram"]
    CH = 512
    l0 = variant == "l0"
    pre = variant + "_"

    grow_d = nc.dram_tensor(pre + "grow_d", [8, S], BF16,
                            kind="ExternalOutput" if io.get("dbg") else "Internal").ap()
    P.dma("pool", acc_d, s_f32)

    stA, sal = io["scope"]()
    xb = [sal(pre + f"ma_xb{i}", [128, 8, CH], BF16) for i in range(2)]
    rows = [sal(pre + f"ma_rows{i}", [1, CH], F32) for i in range(6)]
    sqbf = sal(pre + "ma_sqbf", [128, CH], BF16)
    gsc = {k: sal(pre + f"ma_{k}", [128, 8], F32) for k in ("lg", "mx", "gc", "e1", "e2")}
    gsc["gates_bf"] = sal(pre + "ma_gbf", [128, 8], BF16)
    gtmp = sal(pre + "ma_gtmp", [8, 128], BF16)
    if l0:
        xfs = sal(pre + "ma_xf", [128, 8, CH], F32)
        xfb = sal(pre + "ma_xfb", [128, 8, CH], BF16)
        xhb = sal(pre + "ma_xhb", [128, 8, CH], BF16)
        gcol = sal(pre + "ma_gcol", [128, 8], F32)
        bcol = sal(pre + "ma_bcol", [128, 8], F32)
        P.dma("sp", gcol[:], io["ln_g"])
        P.dma("sp", bcol[:], io["ln_b"])
        rw_sb = sal(pre + "ma_rw", [128, 8, 8], F32)
        P.dma("sp", rw_sb[:], io["router_w"].rearrange("(it p) e -> p it e", p=128))
    else:
        xf32 = [sal(pre + f"ma_x32{i}", [128, 8, CH], F32) for i in range(2)]
        gw_sb = sal(pre + "ma_gw", [128, 8, 8], F32)
        P.dma("sp", gw_sb[:], io["gate_wt"].rearrange("(it p) e -> p it e", p=128))

    for c in range(S // CH):
        cb = c % 2
        cs = slice(CH * c, CH * (c + 1))
        P.dma("sp", xb[cb][:], s_bf[:, cs].rearrange("(it p) t -> p it t", p=128))
        if l0:
            emit_stats(P, [xb[cb][:, i, :] for i in range(8)], sqbf[:], ones_bf,
                       bank[2][0:1, 0:CH], bank[3][0:1, 0:CH])
            emit_ln_rows(P, bank[2][0:1, 0:CH], bank[3][0:1, 0:CH],
                         [rows[j][:] for j in range(4)])
            P.mm(bank[4][:, 0:CH], ones_row, rows[0][0:1, 0:CH])
            P.mm(bank[5][:, 0:CH], ones_row, rows[1][0:1, 0:CH])
            for i in range(8):
                P.tt("dve", xfs[:, i, :], xb[cb][:, i, :], bank[4][:, 0:CH], ALU.subtract)
                P.tt("dve", xfs[:, i, :], xfs[:, i, :], bank[5][:, 0:CH], ALU.mult)
                P.ts("dve", xfs[:, i, :], xfs[:, i, :], gcol[:, i:i + 1], bcol[:, i:i + 1],
                     ALU.mult, ALU.add)
                P.cp("dve", xfb[:, i, :], xfs[:, i, :])
            for t4 in range(4):
                lgp = bank[2][0:128, 0:8]
                for i in range(8):
                    P.mm(lgp, xfs[:, i, t4 * 128:(t4 + 1) * 128], rw_sb[:, i, :],
                         start=(i == 0), stop=(i == 7))
                emit_gates(P, io, lgp, gsc)
                gtp = bank[3][:].bitcast(BF16)[0:8, 0:128]
                P.transp(gtp, gsc["gates_bf"][:], ident)
                P.cp("act", gtmp[:], gtp)
                P.dma("act", grow_d[0:8, CH * c + 128 * t4: CH * c + 128 * (t4 + 1)], gtmp[:])
                if io.get("dbg") == "phaseA" and c == 0 and t4 == 0:
                    for nm in ("lg", "mx", "gc", "e1", "e2"):
                        P.dma("act", io["dbg_" + nm], gsc[nm][:])
                    P.dma("act", io["dbg_gbf"], gsc["gates_bf"][:])
            emit_stats(P, [xfb[:, i, :] for i in range(8)], sqbf[:], ones_bf,
                       bank[2][0:1, 0:CH], bank[3][0:1, 0:CH])
            emit_ln_rows(P, bank[2][0:1, 0:CH], bank[3][0:1, 0:CH],
                         [rows[j][:] for j in (4, 5, 2, 3)])
            P.mm(bank[4][:, 0:CH], ones_row, rows[4][0:1, 0:CH])
            P.mm(bank[5][:, 0:CH], ones_row, rows[5][0:1, 0:CH])
            for i in range(8):
                P.tt("dve", xfs[:, i, :], xfs[:, i, :], bank[4][:, 0:CH], ALU.subtract)
                P.tt("dve", xhb[:, i, :], xfs[:, i, :], bank[5][:, 0:CH], ALU.mult)
            P.dma("act", io["xhat_dram"][:, cs].rearrange("(it p) t -> p it t", p=128), xhb[:])
        else:
            P.dma("sp", xf32[cb][:], s_f32[:, cs].rearrange("(it p) t -> p it t", p=128))
            for t4 in range(4):
                lgp = bank[2][0:128, 0:8]
                for i in range(8):
                    P.mm(lgp, xf32[cb][:, i, t4 * 128:(t4 + 1) * 128], gw_sb[:, i, :],
                         start=(i == 0), stop=(i == 7))
                emit_gates(P, io, lgp, gsc)
                gtp = bank[3][:].bitcast(BF16)[0:8, 0:128]
                P.transp(gtp, gsc["gates_bf"][:], ident)
                P.cp("act", gtmp[:], gtp)
                P.dma("act", grow_d[0:8, CH * c + 128 * t4: CH * c + 128 * (t4 + 1)], gtmp[:])

    P.barrier()
    stA.close()
    if io.get("dbg") == "phaseA":
        return

    SL = 1024
    NSL = S // SL
    stB, sal = io["scope"]()
    xh = sal(pre + "me_xh", [128, 8, SL], BF16)
    w1c = [sal(pre + f"me_w1c{i}", [128, 8, 128], BF16) for i in range(2)]
    w2c = [sal(pre + f"me_w2c{i}", [128, 8, 128], BF16) for i in range(2)]
    wod = [sal(pre + f"me_wod{i}", [128, 32, 128], BF16) for i in range(2)]
    hid = sal(pre + "me_hid", [128, 32, SL], BF16)
    xin = sal(pre + "me_xin", [128, 8, SL], BF16)
    x2c = sal(pre + "me_x2c", [128, CH], F32)
    x2s = sal(pre + "me_x2s", [128, CH], F32)
    tsb = sal(pre + "me_tsb", [128, CH], F32)
    gbsb = sal(pre + "me_gb", [128, 2, CH], F32)
    growrow = sal(pre + "me_grow", [1, 2, CH], BF16)
    outsb = sal(pre + "me_out", [128, 8, SL], F32)
    if l0:
        egc = sal(pre + "me_eg", [128, 8, 8], F32)
        ebc = sal(pre + "me_eb", [128, 8, 8], F32)
        P.dma("sp", egc[:], io["eg"])
        P.dma("sp", ebc[:], io["eb"])
        b1c = sal(pre + "me_b1", [128, 32, 8], F32)
        b2c = sal(pre + "me_b2", [128, 32, 8], F32)
        boc = sal(pre + "me_bo", [128, 8, 8], F32)
        P.dma("sp", b1c[:], io["b1"])
        P.dma("sp", b2c[:], io["b2"])
        P.dma("sp", boc[:], io["bo"])

    wkeys = {"l0": ("w1", "w2", "wo"), "l1": ("w1t", "w3t", "w2t")}[variant]
    for sl in range(NSL):
        ss = slice(SL * sl, SL * (sl + 1))
        src = io["xhat_dram"] if l0 else s_bf
        P.dma("sp", xh[:], src[:, ss].rearrange("(it p) t -> p it t", p=128))
        for e in range(8):
            if l0:
                for i in range(8):
                    P.ts("dve", xin[:, i, :], xh[:, i, :], egc[:, i, e:e + 1],
                         ebc[:, i, e:e + 1], ALU.mult, ALU.add)
                xi_src = xin
            else:
                xi_src = xh
            P.dma("act", growrow[:], grow_d[e:e + 1, ss].rearrange("r (a b) -> r a b", a=2))
            for tt in range(2):
                P.mm(bank[6 + tt][:, 0:CH], ones_row_bf, growrow[0:1, tt, :])
                P.cp("dve", gbsb[:, tt, :], bank[6 + tt][:, 0:CH])
            for ht in range(32):
                hb = ht % 2
                P.dma("pool", w1c[hb][:], io[wkeys[0]][e][:, ht * 128:(ht + 1) * 128]
                      .rearrange("(it p) h -> p it h", p=128))
                P.dma("pool", w2c[hb][:], io[wkeys[1]][e][:, ht * 128:(ht + 1) * 128]
                      .rearrange("(it p) h -> p it h", p=128))
                for tt in range(2):
                    ts_ = slice(CH * tt, CH * (tt + 1))
                    pA = bank[2 * hb][:, 0:CH]
                    pB = bank[2 * hb + 1][:, 0:CH]
                    for i in range(8):
                        P.mm(pA, w1c[hb][:, i, :], xi_src[:, i, ts_],
                             start=(i == 0), stop=(i == 7))
                    for i in range(8):
                        P.mm(pB, w2c[hb][:, i, :], xi_src[:, i, ts_],
                             start=(i == 0), stop=(i == 7))
                    if l0:
                        P.act(x2c[:], pB, AF.Identity, bias=b2c[:, ht, e:e + 1])
                        P.ts("dve", x2c[:], x2c[:], 20.0, -20.0, ALU.min, ALU.max)
                        P.act(x2s[:], x2c[:], AF.Silu)
                        P.stt("dve", tsb[:], pA, b1c[:, ht, e:e + 1], x2s[:],
                              ALU.add, ALU.mult)
                        P.ts("dve", hid[:, ht, ts_], tsb[:], 10000.0, -10000.0,
                             ALU.min, ALU.max)
                    else:
                        P.act(x2s[:], pA, AF.Silu)
                        P.tt("dve", hid[:, ht, ts_], pB, x2s[:], ALU.mult)
            for dt in range(8):
                db = dt % 2
                P.dma("pool", wod[db][:], io[wkeys[2]][e][:, dt * 128:(dt + 1) * 128]
                      .rearrange("(ht p) o -> p ht o", p=128))
                for tt in range(2):
                    ts_ = slice(CH * tt, CH * (tt + 1))
                    po = bank[4 + (2 * db + tt) % 2][:, 0:CH] if False else bank[4 + tt][:, 0:CH]
                    for ht in range(32):
                        P.mm(po, wod[db][:, ht, :], hid[:, ht, ts_],
                             start=(ht == 0), stop=(ht == 31))
                    if l0:
                        P.ts("dve", tsb[:], po, boc[:, dt, e:e + 1], 10000.0,
                             ALU.add, ALU.min)
                        P.stt("dve", outsb[:, dt, ts_], tsb[:], -10000.0,
                              gbsb[:, tt, :], ALU.max, ALU.mult)
                    else:
                        P.tt("dve", outsb[:, dt, ts_], po, gbsb[:, tt, :], ALU.mult)
            P.dma("pool", acc_d[:, ss].rearrange("(it p) t -> p it t", p=128),
                  outsb[:], accum=True)

    P.barrier()
    stB.close()

    stC, sal = io["scope"]()
    yx = sal(pre + "mf_yx", [128, 8, CH], F32)
    lnbf = sal(pre + "mf_lnbf", [128, 8, CH], BF16)
    sqb2 = sal(pre + "mf_sq", [128, CH], BF16)
    rows2 = [sal(pre + f"mf_rows{i}", [1, CH], F32) for i in range(4)]
    obf = sal(pre + "mf_obf", [128, CH], BF16)
    for c in range(S // CH):
        cs = slice(CH * c, CH * (c + 1))
        P.dma("sp", yx[:], acc_d[:, cs].rearrange("(it p) t -> p it t", p=128))
        emit_ln_store(P, io, yx, cs, lnbf, sqb2, rows2, obf,
                      io["d_f32"], io["d_bf"], CH)
    P.barrier()
    stC.close()



def emit_l1_attn(P, io):
    nc = P.nc
    bank = io["banks"]
    ident = io["ident"]
    ones_row = io["ones_row"]
    ones_bf = io["ones_bf"]
    s_f32, s_bf = io["s_f32"], io["s_bf"]
    GCH = 1024

    st0, sal = io["scope"]()
    fw = sal("t_fw", [1, 4], F32)
    w01b = sal("t_w01", [128, 2], F32)
    P.dma("sp", fw[0:1, 0:2], io["fusion"])
    P.tt("dve", fw[0:1, 2:3], fw[0:1, 0:1], fw[0:1, 1:2], ALU.subtract)
    P.act(fw[0:1, 0:1], fw[0:1, 2:3], AF.Sigmoid)
    P.act(fw[0:1, 1:2], fw[0:1, 2:3], AF.Sigmoid, scale=-1.0)
    P.mm(bank[0][:, 0:2], ones_row, fw[0:1, 0:2])
    P.cp("dve", w01b[:], bank[0][:, 0:2])
    gam = sal("ts_gam", [128, 8], F32)
    bet = sal("ts_bet", [128, 8], F32)
    P.dma("sp", gam[:], io["gam"])
    P.dma("sp", bet[:], io["bet"])
    mask_l = sal("t_maskl", [128, 128], F32)
    P.dma("sp", mask_l[:], io["mask_l"])

    lo_c = sal("t_lo", [128, 8, GCH], BF16)
    fu = sal("t_fu", [128, 8, 512], F32)
    xf32 = sal("t_xf32", [128, 8, 512], F32)
    rows = [sal(f"t_rows{i}", [1, 512], F32) for i in range(4)]
    lnbf = sal("t_lnbf", [128, 8, 512], BF16)
    sqbf = sal("t_sqbf", [128, 512], BF16)
    obf = sal("t_obf", [128, 512], BF16)
    srw = sal("t_srw", [128, 4], F32)

    def path_build_attend(local, c, stP, qg, kg, vt, hh):
        pfx = "tl" if local else "tg"
        cs = slice(GCH * c, GCH * (c + 1))
        wi = io["wl_int"] if local else io["wg_int"]
        stB, salB = io["scope"]()
        xb = salB(pfx + "_xb", [128, 8, GCH], BF16)
        proj_sb = salB(pfx + "_proj", [128, 8, 1024], BF16)
        win_sb = salB(pfx + "_win", [128, 8, 1024], BF16)
        binc = salB(pfx + "_binc", [128, 24], F32)
        bvb = salB(pfx + "_bvb", [128, 1024], F32)
        bvr = salB(pfx + "_bvr", [1, 512], F32)
        pq = salB(pfx + "_pq", [128, 8, GCH], BF16)
        P.dma("sp", xb[:], s_bf[:, cs].rearrange("(it p) t -> p it t", p=128))
        P.dma("pool", proj_sb[:], (io["pl"] if local else io["pg"])
              .rearrange("(it p) o -> p it o", p=128))
        P.dma("sp", binc[:], io["blin_col"] if local else io["bgin_col"])
        bvsrc = io["blv_row"] if local else io["bgv_row"]
        P.dma("sp", bvr[:], bvsrc[0:1, 0:512])
        P.mm(bank[2][:, 0:512], ones_row, bvr[0:1, 0:512])
        P.cp("dve", bvb[:, 0:512], bank[2][:, 0:512])
        P.dma("sp", bvr[:], bvsrc[0:1, 512:1024])
        P.mm(bank[3][:, 0:512], ones_row, bvr[0:1, 0:512])
        P.cp("dve", bvb[:, 512:1024], bank[3][:, 0:512])
        for ot in range(8):
            for nt in range(2):
                ps = bank[nt % 2]
                nsl = slice(512 * nt, 512 * (nt + 1))
                for i in range(8):
                    P.mm(ps[:, 0:512], proj_sb[:, i, ot * 128:(ot + 1) * 128],
                         xb[:, i, nsl], start=(i == 0), stop=(i == 7))
                P.cp("act", pq[0:128, ot, nsl], ps[:, 0:512])
        for part, dst in ((0, qg), (1, kg), (2, None)):
            P.dma("pool", win_sb[:], wi[:, 1024 * part:1024 * (part + 1)]
                  .rearrange("(it p) o -> p it o", p=128))
            if part < 2:
                for ot in range(8):
                    for nt in range(2):
                        ps = bank[nt % 2]
                        nsl = slice(512 * nt, 512 * (nt + 1))
                        for i in range(8):
                            P.mm(ps[:, 0:512], win_sb[:, i, ot * 128:(ot + 1) * 128],
                                 pq[:, i, nsl], start=(i == 0), stop=(i == 7))
                        P.act(dst[0:128, ot, nsl], ps[:, 0:512], AF.Identity,
                              bias=binc[:, 8 * part + ot:8 * part + ot + 1])
            else:
                for twt in range(8):
                    t0 = twt * 128
                    ps = bank[twt % 2]
                    for half in range(2):
                        hsl = slice(512 * half, 512 * (half + 1))
                        for i in range(8):
                            P.mm(ps[:, 0:512], pq[:, i, t0:t0 + 128],
                                 win_sb[:, i, hsl],
                                 start=(i == 0), stop=(i == 7))
                        P.tt("dve", vt[:, twt, hsl], ps[:, 0:512], bvb[:, hsl], ALU.add)
        P.barrier()
        stB.close()
        stA, salA = io["scope"]()
        ss = salA(pfx + "_ss", [128, 1024], F32)[:]
        pex = salA(pfx + "_pex", [128, 1024], F32)[:]
        pbf = salA(pfx + "_pbf", [128, 1024], BF16)[:]
        ptg = salA(pfx + "_ptg", [128, 1024], BF16)[:]
        mgs = [salA(pfx + f"_mg{i}", [128, 2, 512], F32) for i in range(2)]
        W = 128 if local else 1024
        swid = W
        for wq in range(8):
            if not local:
                mg = mgs[wq % 2]
                P.dma("sp", mg[:], io["mask_g"][wq * 128:(wq + 1) * 128, :]
                      .rearrange("p (a b) -> p a b", a=2))
            for h in range(16):
                hb = (h % 2) * 64
                q0 = wq * 128
                qsl = qg[hb:hb + 64, h // 2, q0:q0 + 128]
                sc = [bank[2], bank[3]]
                if local:
                    P.mm(sc[0][:, 0:128], qsl, kg[hb:hb + 64, h // 2, q0:q0 + 128])
                    P.stt("dve", ss[:, 0:128], sc[0][:, 0:128], 0.125, mask_l[:],
                          ALU.mult, ALU.add)
                else:
                    for kt2 in range(2):
                        P.mm(sc[kt2][:, 0:512], qsl,
                             kg[hb:hb + 64, h // 2, 512 * kt2:512 * (kt2 + 1)])
                    P.stt("dve", ss[:, 0:512], sc[0][:, 0:512], 0.125, mg[:, 0, :],
                          ALU.mult, ALU.add)
                    P.stt("dve", ss[:, 512:1024], sc[1][:, 0:512], 0.125, mg[:, 1, :],
                          ALU.mult, ALU.add)
                P.red(srw[:, 0:1], ss[:, 0:swid], is_max=True, negate=True)
                P.act(pex[:, 0:swid], ss[:, 0:swid], AF.Exp, bias=srw[:, 0:1],
                      accum=srw[:, 1:2])
                P.recip(srw[:, 2:3], srw[:, 1:2])
                P.ts("dve", pbf[:, 0:swid], pex[:, 0:swid], srw[:, 2:3], None, ALU.mult)
                bT = bank[4][:].bitcast(BF16)
                for j in range(swid // 128):
                    P.transp(bT[:, 128 * j:128 * (j + 1)],
                             pbf[:, 128 * j:128 * (j + 1)], ident)
                P.cp("act", ptg[:, 0:swid], bT[:, 0:swid])
                ho = bank[6 + (h % 2)][hb:hb + 64, 0:128]
                for j in range(swid // 128):
                    P.mm(ho, vt[:, wq if local else j, 64 * h:64 * h + 64],
                         ptg[:, 128 * j:128 * (j + 1)],
                         start=(j == 0), stop=(j == swid // 128 - 1))
                P.cp("act", hh[hb:hb + 64, h // 2, q0:q0 + 128], ho)
        P.barrier()
        stA.close()

    for c in range(S // GCH):
        stP, salP = io["scope"]()
        qg = salP("p_q", [128, 8, GCH], BF16)
        kg = salP("p_k", [128, 8, GCH], BF16)
        vt = salP("p_vt", [128, 8, 1024], BF16)
        hh = salP("p_hh", [128, 8, GCH], BF16)
        path_build_attend(True, c, stP, qg, kg, vt, hh)
        stO, salO = io["scope"]()
        wout_sb = salO("tl_wout", [128, 8, 1024], BF16)
        boutc = salO("tl_bout", [128, 8], F32)
        P.dma("pool", wout_sb[:], io["wl_outt"].rearrange("(it p) o -> p it o", p=128))
        P.dma("sp", boutc[:], io["blout_col"])
        for ot in range(8):
            for nt in range(2):
                ps = bank[nt % 2]
                nsl = slice(512 * nt, 512 * (nt + 1))
                for i in range(8):
                    P.mm(ps[:, 0:512], wout_sb[:, i, ot * 128:(ot + 1) * 128],
                         hh[:, i, nsl], start=(i == 0), stop=(i == 7))
                P.act(lo_c[0:128, ot, nsl], ps[:, 0:512], AF.Identity,
                      bias=boutc[:, ot:ot + 1])
        P.barrier()
        stO.close()
        stP.close()
        stP, salP = io["scope"]()
        qg = salP("p_q", [128, 8, GCH], BF16)
        kg = salP("p_k", [128, 8, GCH], BF16)
        vt = salP("p_vt", [128, 8, 1024], BF16)
        hh = salP("p_hh", [128, 8, GCH], BF16)
        path_build_attend(False, c, stP, qg, kg, vt, hh)
        stO, salO = io["scope"]()
        wout_sb = salO("tg_wout", [128, 8, 1024], BF16)
        boutc = salO("tg_bout", [128, 8], F32)
        got = salO("tg_got", [128, 512], F32)
        t2 = salO("tg_t2", [128, 512], F32)
        P.dma("pool", wout_sb[:], io["wg_outt"].rearrange("(it p) o -> p it o", p=128))
        P.dma("sp", boutc[:], io["bgout_col"])
        for half in range(2):
            hsl = slice(512 * half, 512 * (half + 1))
            colslice = slice(GCH * c + 512 * half, GCH * c + 512 * (half + 1))
            P.dma("sp", xf32[:], s_f32[:, colslice].rearrange("(it p) t -> p it t", p=128))
            for ot in range(8):
                ps = bank[ot % 2]
                for i in range(8):
                    P.mm(ps[:, 0:512], wout_sb[:, i, ot * 128:(ot + 1) * 128],
                         hh[:, i, hsl], start=(i == 0), stop=(i == 7))
                P.act(got[:], ps[:, 0:512], AF.Identity, bias=boutc[:, ot:ot + 1])
                P.act(t2[:], got[:], AF.Identity, scale=w01b[:, 1:2])
                P.act(got[:], lo_c[:, ot, hsl], AF.Identity, scale=w01b[:, 0:1])
                P.tt("dve", fu[:, ot, :], got[:], t2[:], ALU.add)
            for i in range(8):
                P.cp("dve", lnbf[:, i, :], fu[:, i, :])
            emit_stats(P, [lnbf[:, i, :] for i in range(8)], sqbf[:], ones_bf,
                       bank[2][0:1, 0:512], bank[3][0:1, 0:512])
            emit_ln_rows(P, bank[2][0:1, 0:512], bank[3][0:1, 0:512],
                         [r[:] for r in rows])
            P.mm(bank[4][:, 0:512], ones_row, rows[0][0:1, 0:512])
            P.mm(bank[5][:, 0:512], ones_row, rows[1][0:1, 0:512])
            for i in range(8):
                P.tt("dve", fu[:, i, :], fu[:, i, :], bank[4][:, 0:512], ALU.subtract)
                P.tt("dve", fu[:, i, :], fu[:, i, :], bank[5][:, 0:512], ALU.mult)
                P.ts("dve", fu[:, i, :], fu[:, i, :], gam[:, i:i + 1], bet[:, i:i + 1],
                     ALU.mult, ALU.add)
                P.tt("dve", fu[:, i, :], fu[:, i, :], xf32[:, i, :], ALU.add)
            emit_ln_store(P, io, fu, colslice, lnbf, sqbf, rows, obf,
                          io["d_f32"], io["d_bf"], 512)
        P.barrier()
        stO.close()
        stP.close()
    P.barrier()
    st0.close()



def emit_l0_attn(P, io):
    nc = P.nc
    CH = 512
    NCH = S // CH
    sal = io["salloc"]

    wq_sb = sal("wq_sb", [128, 8, 1024], BF16)
    wk_sb = sal("wk_sb", [128, 8, 256], BF16)
    wv_sb = sal("wv_sb", [128, 8, 256], BF16)
    wo_sb = sal("wo_sb", [128, 8, 1024], BF16)
    for sb, src in ((wq_sb, io["wq"]), (wk_sb, io["wk"]), (wv_sb, io["wv"]), (wo_sb, io["wo"])):
        P.dma("pool", sb[:], src.rearrange("(it p) o -> p it o", p=128))
    mloc = sal("mloc", [128, 192], F32)
    mloc0 = sal("mloc0", [128, 192], F32)
    manch = sal("manch", [128, 32, 64], F32)
    P.dma("sp", mloc[:], io["mloc"])
    P.dma("sp", mloc0[:], io["mloc0"])
    P.dma("sp", manch[:].rearrange("p a b -> p (a b)"), io["manch"])

    ident = io["ident"]
    ones_bf = io["ones_bf"]
    ones_row = io["ones_row"]

    xb = [sal(f"xb{i}", [128, 8, CH], BF16) for i in range(2)]
    xf = [sal(f"xf{i}", [128, 8, CH], F32) for i in range(2)]
    qc = sal("qc", [64, 16, CH], BF16)
    kc = sal("kc", [64, 4, CH], BF16)
    vt = sal("vt", [128, 4, 256], BF16)
    ktail = sal("ktail", [64, 4, 64], BF16)
    vttail = sal("vttail", [128, 256], BF16)
    ka = sal("ka", [64, 4, 64], BF16)
    vta = sal("vta", [128, 256], BF16)
    ao = sal("ao", [128, 8, CH], BF16)
    s_sb = [sal(f"s_sb{i}", [128, 256], F32) for i in range(2)]
    pex = [sal(f"pex{i}", [128, 256], F32) for i in range(2)]
    pbf = [sal(f"pbf{i}", [128, 256], BF16) for i in range(2)]
    ptsb = [sal(f"ptsb{i}", [128, 256], BF16) for i in range(2)]
    pt2sb = [sal(f"pt2sb{i}", [128, 128], BF16) for i in range(2)]
    srows = [sal(f"srows{i}", [128, 4], F32) for i in range(2)]
    yx = sal("yx", [128, 8, CH], F32)
    lnbf = sal("lnbf", [128, 8, CH], BF16)
    sqbf = sal("sqbf", [128, CH], BF16)
    rows = [sal(f"rows{i}", [1, CH], F32) for i in range(4)]
    obf = sal("obf", [128, CH], BF16)

    bank = io["banks"]
    P.memset("dve", ka[:], 0.0)
    P.memset("dve", vta[:], 0.0)
    P.memset("dve", ktail[:], 0.0)
    P.memset("dve", vttail[:], 0.0)

    for c in range(NCH):
        cb = c % 2
        colslice = slice(CH * c, CH * (c + 1))
        P.dma("sp", xb[cb][:], io["s_bf"][:, colslice].rearrange("(it p) t -> p it t", p=128))
        P.dma("sp", xf[cb][:], io["s_f32"][:, colslice].rearrange("(it p) t -> p it t", p=128))

        for h in range(16):
            ps = bank[h % 2]
            for i in range(8):
                P.mm(ps[0:64, 0:CH], wq_sb[:, i, h * 64:(h + 1) * 64], xb[cb][:, i, :],
                     start=(i == 0), stop=(i == 7))
            P.cp("act", qc[0:64, h, :], ps[0:64, 0:CH])
        for g in range(4):
            ps = bank[g % 2]
            for i in range(8):
                P.mm(ps[0:64, 0:CH], wk_sb[:, i, g * 64:(g + 1) * 64], xb[cb][:, i, :],
                     start=(i == 0), stop=(i == 7))
            P.cp("act", kc[0:64, g, :], ps[0:64, 0:CH])
        for t4 in range(4):
            ps = bank[t4 % 2]
            for i in range(8):
                P.mm(ps[:, 0:256], xb[cb][:, i, t4 * 128:(t4 + 1) * 128], wv_sb[:, i, :],
                     start=(i == 0), stop=(i == 7))
            P.cp("act", vt[:, t4, :], ps[:, 0:256])

        dbg = io.get("dbg")
        if dbg and c == 0:
            P.dma("act", dbg["q"], qc[0:64, :, :].rearrange("p a b -> p (a b)"))
            P.dma("act", dbg["k"], kc[0:64, :, :].rearrange("p a b -> p (a b)"))
            P.dma("act", dbg["vt"], vt[:].rearrange("p a b -> p (a b)"))

        P.cp("dve", ka[0:64, :, 8 * c:8 * c + 8], kc[0:64, :, 63::64])
        for j in range(8):
            P.dma("act", vta[8 * c + j:8 * c + j + 1, :],
                  vt[63 + 64 * (j % 2):64 + 64 * (j % 2), j // 2, :])

        for sub in range(4):
            cp_ = 4 * c + sub
            for h in range(16):
                sc = bank[2 + (h % 2)]
                g = h // 4
                qT = qc[0:64, h, sub * 128:(sub + 1) * 128]
                if sub == 0:
                    P.mm(sc[:, 0:64], qT, ktail[0:64, g, :])
                    P.mm(sc[:, 64:192], qT, kc[0:64, g, 0:128])
                else:
                    P.mm(sc[:, 0:192], qT, kc[0:64, g, sub * 128 - 64:sub * 128 + 128])
                P.mm(sc[:, 192:256], qT, ka[0:64, g, :])
                ml = mloc0 if cp_ == 0 else mloc
                ss = s_sb[h % 2]
                P.stt("dve", ss[:, 0:192], sc[:, 0:192], 0.125, ml[:], ALU.mult, ALU.add)
                P.stt("dve", ss[:, 192:256], sc[:, 192:256], 0.125, manch[:, cp_, :], ALU.mult, ALU.add)
                rw = srows[h % 2]
                P.red(rw[:, 0:1], ss[:], is_max=True, negate=True)
                P.act(pex[h % 2][:], ss[:], AF.Exp, bias=rw[:, 0:1], accum=rw[:, 1:2])
                P.recip(rw[:, 2:3], rw[:, 1:2])
                P.ts("dve", pbf[h % 2][:], pex[h % 2][:], rw[:, 2:3], None, ALU.mult)
                bT1 = bank[4 + (h % 2)][:].bitcast(BF16)
                bT2 = bank[6][:].bitcast(BF16)
                pb = pbf[h % 2]
                P.transp(bT1[64:128, 0:128], pb[:, 0:64], ident, tile_position=(0, 64))
                P.transp(bT1[0:128, 128:256], pb[:, 64:192], ident)
                P.transp(bT2[0:64, 0:128], pb[:, 192:256], ident)
                P.cp("act", ptsb[h % 2][64:128, 0:128], bT1[64:128, 0:128])
                P.cp("act", ptsb[h % 2][:, 128:256], bT1[:, 128:256])
                P.cp("act", pt2sb[h % 2][0:64, :], bT2[0:64, 0:128])
                hb = (h % 2) * 64
                ho = bank[7][hb:hb + 64, 0:128]
                if sub == 0:
                    vA = vttail[64:128, g * 64:(g + 1) * 64]
                else:
                    vA = vt[64:128, sub - 1, g * 64:(g + 1) * 64]
                vB = vt[:, sub, g * 64:(g + 1) * 64]
                P.mm(ho, vA, ptsb[h % 2][64:128, 0:128], start=True, stop=False)
                P.mm(ho, vB, ptsb[h % 2][:, 128:256], start=False, stop=False)
                P.mm(ho, vta[0:64, g * 64:(g + 1) * 64], pt2sb[h % 2][0:64, :],
                     start=False, stop=True)
                P.cp("act", ao[hb:hb + 64, h // 2, sub * 128:(sub + 1) * 128], ho)
                if dbg and c == 0 and sub == 1 and h == 5:
                    P.dma("act", dbg["ss"], s_sb[h % 2][:])
                    P.dma("act", dbg["p"], pbf[h % 2][:])

        P.cp("dve", ktail[0:64, :, :], kc[0:64, :, CH - 64:CH])
        P.cp("dve", vttail[64:128, :], vt[64:128, 3, :])

        if dbg and c == 0:
            P.dma("act", dbg["ao"], ao[:].rearrange("p a b -> p (a b)"))

        for ot in range(8):
            ps = bank[ot % 2]
            for i in range(8):
                P.mm(ps[:, 0:CH], wo_sb[:, i, ot * 128:(ot + 1) * 128], ao[:, i, :],
                     start=(i == 0), stop=(i == 7))
            P.tt("dve", yx[:, ot, :], ps[:, 0:CH], xf[cb][:, ot, :], ALU.add)

        emit_ln_store(P, io, yx, colslice, lnbf, sqbf, rows, obf,
                      io["d_f32"], io["d_bf"], CH)



CONST_VALS = [EPS, 1.0, 0.0, -1.0]


def make_core_consts(nc, P, salloc):
    ident_np = np.eye(128, dtype=BF)
    ones_bf_np = np.ones((128, 1), dtype=BF)
    ones_row_np = np.ones((1, 128), np.float32)
    ident = salloc("ident", [128, 128], BF16)
    ones_bf = salloc("ones_bf", [128, 1], BF16)
    ones_row = salloc("ones_row", [1, 128], F32)
    ones_row_bf = salloc("ones_row_bf", [1, 128], BF16)
    P.dma("sp", ident[:], nc.inline_tensor(ident_np, name="ident_c").ap())
    P.dma("sp", ones_bf[:], nc.inline_tensor(ones_bf_np, name="onesbf_c").ap())
    P.dma("sp", ones_row[:], nc.inline_tensor(ones_row_np, name="onesrow_c").ap())
    P.dma("sp", ones_row_bf[:], nc.inline_tensor(ones_bf_np.reshape(1, 128), name="onesrowbf_c").ap())
    cvals = np.tile(np.asarray(CONST_VALS, np.float32)[None, :], (128, 1))
    csb = salloc("const_sb", [128, len(CONST_VALS)], F32)
    P.dma("sp", csb[:], nc.inline_tensor(cvals, name="cvals_c").ap())
    guard = salloc("guard_sb", [1, 8], F32)
    P.dma("sp", guard[:], nc.inline_tensor(np.zeros((1, 8), np.float32), name="guard_c").ap())
    P.guard = guard[:]
    for j, v in enumerate(CONST_VALS):
        nc.const_aps.register(F32, float(v), csb[:, j:j + 1])
    for e in ENGS:
        P._wait(e, "dsp", P.cnt["dsp"])
    return {"ident": ident[:], "ones_bf": ones_bf[:], "ones_row": ones_row[:],
            "ones_row_bf": ones_row_bf[:], "const_sb": csb[:]}


def base_nc():
    nc = bass.Bass("TRN2", target_bir_lowering=False, debug=False)
    import contextlib
    outer = contextlib.ExitStack()

    uid = [0]

    def scope():
        st = contextlib.ExitStack()

        def sal(name, shape, dtype):
            uid[0] += 1
            return st.enter_context(nc.sbuf_tensor(f"{name}_u{uid[0]}", shape, dtype))
        return st, sal

    def salloc(name, shape, dtype):
        return outer.enter_context(nc.sbuf_tensor(name, shape, dtype))
    banks = [outer.enter_context(nc.psum_tensor(f"bank{i}", [128, 512], F32))
             for i in range(8)]
    P = Prog(nc)
    io = {"salloc": salloc, "scope": scope, "banks": [b[:] for b in banks]}
    io.update(make_core_consts(nc, P, salloc))
    return nc, P, io, outer


def moe_dram(nc, io, variant, kind="ExternalInput", dbg_kind_internal="Internal"):
    t = lambda n, shp: nc.dram_tensor(variant + "_" + n, shp, F32, kind=kind).ap()
    if variant == "l0":
        io.update(w1=t("w1", [8, 1024, 4096]), w2=t("w2", [8, 1024, 4096]),
                  wo=t("wo", [8, 4096, 1024]),
                  ln_g=t("ln_g", [128, 8]), ln_b=t("ln_b", [128, 8]),
                  router_w=t("router_w", [1024, 8]),
                  eg=t("eg", [128, 8, 8]), eb=t("eb", [128, 8, 8]),
                  b1=t("b1", [128, 32, 8]), b2=t("b2", [128, 32, 8]),
                  bo=t("bo", [128, 8, 8]))
        io["xhat_dram"] = nc.dram_tensor("l0_xhat", [D, S], BF16, kind=dbg_kind_internal).ap()
    else:
        io.update(w1t=t("w1t", [8, 1024, 4096]), w3t=t("w3t", [8, 1024, 4096]),
                  w2t=t("w2t", [8, 4096, 1024]), gate_wt=t("gate_wt", [1024, 8]))


def host_moe_inputs(variant, rd):
    o = {}
    if variant == "l0":
        o["l0_w1"] = rd("L0_ffn_W1")
        o["l0_w2"] = rd("L0_ffn_W2")
        o["l0_wo"] = rd("L0_ffn_Wo")
        o["l0_ln_g"] = np.ascontiguousarray(rd("L0_ffn_ln_g").reshape(8, 128).T)
        o["l0_ln_b"] = np.ascontiguousarray(rd("L0_ffn_ln_b").reshape(8, 128).T)
        o["l0_router_w"] = rd("L0_ffn_router_w")
        o["l0_eg"] = np.ascontiguousarray(rd("L0_ffn_eg").reshape(8, 8, 128).transpose(2, 1, 0))
        o["l0_eb"] = np.ascontiguousarray(rd("L0_ffn_eb").reshape(8, 8, 128).transpose(2, 1, 0))
        o["l0_b1"] = np.ascontiguousarray(rd("L0_ffn_b1").reshape(8, 32, 128).transpose(2, 1, 0))
        o["l0_b2"] = np.ascontiguousarray(rd("L0_ffn_b2").reshape(8, 32, 128).transpose(2, 1, 0))
        o["l0_bo"] = np.ascontiguousarray(rd("L0_ffn_bo").reshape(8, 8, 128).transpose(2, 1, 0))
    else:
        o["l1_w1t"] = np.ascontiguousarray(rd("L1_ffn_w1").transpose(0, 2, 1))
        o["l1_w3t"] = np.ascontiguousarray(rd("L1_ffn_w3").transpose(0, 2, 1))
        o["l1_w2t"] = np.ascontiguousarray(rd("L1_ffn_w2").transpose(0, 2, 1))
        o["l1_gate_wt"] = np.ascontiguousarray(rd("L1_ffn_gate_w").T)
    return o


def build_moe_only(variant, dbg=None):
    nc, P, io, outer = base_nc()
    s_f32 = nc.dram_tensor("x_t", [D, S], F32, kind="ExternalInput").ap()
    s_bf = nc.dram_tensor("x_bf", [D, S], BF16, kind="Internal").ap()
    d_f32 = nc.dram_tensor("y_t", [D, S], F32, kind="ExternalOutput").ap()
    d_bf = nc.dram_tensor("y_bf", [D, S], BF16, kind="Internal").ap()
    dbg_kind = "ExternalOutput" if dbg else "Internal"
    acc = nc.dram_tensor("acc_d", [D, S], F32, kind=dbg_kind).ap()
    io.update(s_f32=s_f32, s_bf=s_bf, d_f32=d_f32, d_bf=d_bf, acc_dram=acc,
              dbg=dbg)
    moe_dram(nc, io, variant, dbg_kind_internal=dbg_kind)
    if dbg == "phaseA":
        for nm in ("lg", "mx", "gc", "e1", "e2"):
            io["dbg_" + nm] = nc.dram_tensor("dbg_" + nm, [128, 8], F32,
                                             kind="ExternalOutput").ap()
        io["dbg_gbf"] = nc.dram_tensor("dbg_gbf", [128, 8], BF16,
                                       kind="ExternalOutput").ap()
    P.dma("pool", s_bf, s_f32)
    emit_moe(P, io, variant)
    P.finalize()
    P.build()
    outer.close()
    return nc


def l1attn_dram(nc, io, kind="ExternalInput"):
    t = lambda n, shp, dt=F32: nc.dram_tensor("t_" + n, shp, dt, kind=kind).ap()
    io.update(pl=t("pl", [D, D]), pg=t("pg", [D, D]),
              wl_int=t("wl_int", [D, 3 * D]), wg_int=t("wg_int", [D, 3 * D]),
              wl_outt=t("wl_outt", [D, D]), wg_outt=t("wg_outt", [D, D]),
              blin_col=t("blin_col", [128, 24]), bgin_col=t("bgin_col", [128, 24]),
              blv_row=t("blv_row", [1, D]), bgv_row=t("bgv_row", [1, D]),
              blout_col=t("blout_col", [128, 8]), bgout_col=t("bgout_col", [128, 8]),
              fusion=t("fusion", [1, 2]), gam=t("gam", [128, 8]), bet=t("bet", [128, 8]))
    mask_l = np.triu(np.ones((128, 128), np.float32), k=1)
    mask_g = np.triu(np.ones((1024, 1024), np.float32), k=1)
    io["mask_l"] = nc.inline_tensor(mask_l, name="maskl_c").ap()
    io["mask_g"] = nc.inline_tensor(mask_g, name="maskg_c").ap()


def host_l1attn_inputs(rd):
    o = {}
    o["t_pl"] = rd("L1_attn_proj_local")
    o["t_pg"] = rd("L1_attn_proj_global")
    o["t_wl_int"] = np.ascontiguousarray(rd("L1_attn_Wl_in").T)
    o["t_wg_int"] = np.ascontiguousarray(rd("L1_attn_Wg_in").T)
    o["t_wl_outt"] = np.ascontiguousarray(rd("L1_attn_Wl_out").T)
    o["t_wg_outt"] = np.ascontiguousarray(rd("L1_attn_Wg_out").T)
    o["t_blin_col"] = np.ascontiguousarray(rd("L1_attn_bl_in").reshape(24, 128).T)
    o["t_bgin_col"] = np.ascontiguousarray(rd("L1_attn_bg_in").reshape(24, 128).T)
    o["t_blv_row"] = np.ascontiguousarray(rd("L1_attn_bl_in")[2048:3072].reshape(1, 1024))
    o["t_bgv_row"] = np.ascontiguousarray(rd("L1_attn_bg_in")[2048:3072].reshape(1, 1024))
    o["t_blout_col"] = np.ascontiguousarray(rd("L1_attn_bl_out").reshape(8, 128).T)
    o["t_bgout_col"] = np.ascontiguousarray(rd("L1_attn_bg_out").reshape(8, 128).T)
    o["t_fusion"] = rd("L1_attn_fusion_w").reshape(1, 2)
    o["t_gam"] = np.ascontiguousarray(rd("L1_attn_ln_gamma").reshape(8, 128).T)
    o["t_bet"] = np.ascontiguousarray(rd("L1_attn_ln_beta").reshape(8, 128).T)
    return o


def host_l0attn_inputs(rd):
    return {
        "wq_t": np.ascontiguousarray(rd("L0_attn_Wq").T),
        "wk_t": np.ascontiguousarray(rd("L0_attn_Wk").T),
        "wv_t": np.ascontiguousarray(rd("L0_attn_Wv").T),
        "wo_t": np.ascontiguousarray(rd("L0_attn_Wo").T),
    }


def build_l1attn_only():
    nc, P, io, outer = base_nc()
    s_f32 = nc.dram_tensor("x_t", [D, S], F32, kind="ExternalInput").ap()
    s_bf = nc.dram_tensor("x_bf", [D, S], BF16, kind="Internal").ap()
    d_f32 = nc.dram_tensor("y_t", [D, S], F32, kind="ExternalOutput").ap()
    d_bf = nc.dram_tensor("y_bf", [D, S], BF16, kind="Internal").ap()
    io.update(s_f32=s_f32, s_bf=s_bf, d_f32=d_f32, d_bf=d_bf)
    l1attn_dram(nc, io)
    P.dma("pool", s_bf, s_f32)
    emit_l1_attn(P, io)
    P.finalize()
    P.build()
    outer.close()
    return nc


def build_l0attn_only(dbg=False):
    nc = bass.Bass("TRN2", target_bir_lowering=False, debug=False)
    import contextlib
    stack = contextlib.ExitStack()

    def salloc(name, shape, dtype):
        return stack.enter_context(nc.sbuf_tensor(name, shape, dtype))

    s_f32 = nc.dram_tensor("x_t", [D, S], F32, kind="ExternalInput").ap()
    s_bf = nc.dram_tensor("x_bf", [D, S], BF16, kind="Internal").ap()
    d_f32 = nc.dram_tensor("y_t", [D, S], F32, kind="ExternalOutput").ap()
    d_bf = nc.dram_tensor("y_bf", [D, S], BF16, kind="Internal").ap()
    wq = nc.dram_tensor("wq_t", [D, D], F32, kind="ExternalInput").ap()
    wk = nc.dram_tensor("wk_t", [D, 256], F32, kind="ExternalInput").ap()
    wv = nc.dram_tensor("wv_t", [D, 256], F32, kind="ExternalInput").ap()
    wo = nc.dram_tensor("wo_t", [D, D], F32, kind="ExternalInput").ap()
    mloc_np, mloc0_np, manch_np = build_l0_masks()
    mloc_d = nc.inline_tensor(mloc_np, name="mloc_c").ap()
    mloc0_d = nc.inline_tensor(mloc0_np, name="mloc0_c").ap()
    manch_d = nc.inline_tensor(manch_np, name="manch_c").ap()

    P = Prog(nc)
    banks = [stack.enter_context(nc.psum_tensor(f"bank{i}", [128, 512], F32)) for i in range(8)]
    consts = make_core_consts(nc, P, salloc)
    P.dma("pool", s_bf, s_f32)
    io = dict(salloc=salloc, banks=[b[:] for b in banks],
              s_f32=s_f32, s_bf=s_bf, d_f32=d_f32, d_bf=d_bf,
              wq=wq, wk=wk, wv=wv, wo=wo,
              mloc=mloc_d, mloc0=mloc0_d, manch=manch_d, **consts)
    if dbg:
        io["dbg"] = {
            "q": nc.dram_tensor("dbg_q", [64, 16 * 512], BF16, kind="ExternalOutput").ap(),
            "k": nc.dram_tensor("dbg_k", [64, 4 * 512], BF16, kind="ExternalOutput").ap(),
            "vt": nc.dram_tensor("dbg_vt", [128, 4 * 256], BF16, kind="ExternalOutput").ap(),
            "ss": nc.dram_tensor("dbg_ss", [128, 256], F32, kind="ExternalOutput").ap(),
            "p": nc.dram_tensor("dbg_p", [128, 256], BF16, kind="ExternalOutput").ap(),
            "ao": nc.dram_tensor("dbg_ao", [128, 8 * 512], BF16, kind="ExternalOutput").ap(),
        }
    emit_l0_attn(P, io)
    P.finalize()
    P.build()
    stack.close()
    return nc


def ref_l0_attn_block(x, Wq, Wk, Wv, Wo):
    import jax
    import jax.numpy as jnp
    import math

    def _ln(v):
        m = jnp.mean(v, axis=-1, keepdims=True)
        s = jnp.var(v, axis=-1, keepdims=True)
        return (v - m) / jnp.sqrt(s + 1e-5)
    B, Sx, Dx = x.shape
    H, H_KV, HEAD_DIM = 16, 4, 64

    def _cron(q, k, v):
        Bq, Hq, Sq, d = q.shape
        w = int(math.ceil(math.sqrt(Sq)))
        scale = 1.0 / math.sqrt(d)
        pos = jnp.arange(Sq)
        off = jnp.arange(w) - (w - 1)
        idx = pos[:, None] + off[None, :]
        loc_valid = idx >= 0
        idx_c = jnp.clip(idx, 0, Sq - 1)
        k_loc = jnp.take(k, idx_c, axis=2)
        v_loc = jnp.take(v, idx_c, axis=2)
        s_loc = jnp.einsum('bhsd,bhswd->bhsw', q, k_loc) * scale
        s_loc = jnp.where(loc_valid[None, None], s_loc, -1e9)
        sidx = jnp.arange(w - 1, Sq, w)
        ks = k[:, :, sidx]
        vs = v[:, :, sidx]
        s_str = jnp.einsum('bhsd,bhnd->bhsn', q, ks) * scale
        str_valid = sidx[None, :] <= pos[:, None] - w
        s_str = jnp.where(str_valid[None, None], s_str, -1e9)
        p = jax.nn.softmax(jnp.concatenate([s_loc, s_str], axis=-1), axis=-1)
        p_loc, p_str = (p[..., :w], p[..., w:])
        out = jnp.einsum('bhsw,bhswd->bhsd', p_loc, v_loc)
        out = out + jnp.einsum('bhsn,bhnd->bhsd', p_str, vs)
        return out
    q = jnp.einsum('bsi,oi->bso', x, Wq).reshape(B, Sx, H, HEAD_DIM).transpose(0, 2, 1, 3)
    k = jnp.einsum('bsi,oi->bso', x, Wk).reshape(B, Sx, H_KV, HEAD_DIM).transpose(0, 2, 1, 3)
    v = jnp.einsum('bsi,oi->bso', x, Wv).reshape(B, Sx, H_KV, HEAD_DIM).transpose(0, 2, 1, 3)
    k = jnp.repeat(k, 4, axis=1)
    v = jnp.repeat(v, 4, axis=1)
    out = _cron(q, k, v)
    out = out.transpose(0, 2, 1, 3).reshape(B, Sx, Dx)
    att = jnp.einsum('bsi,oi->bso', out, Wo)
    return np.asarray(_ln(x + att))



def kernel(**inputs):
    inputs = {k: np.asarray(v) for k, v in inputs.items()}

    def rd(name):
        return inputs[name]

    x = inputs["L0_attn_x"]
    n = 8
    stream = [np.ascontiguousarray(x[b].T) for b in range(n)]

    nc1 = build_l0attn_only()
    w = host_l0attn_inputs(rd)
    res = run_bass_kernel_spmd(nc1, [{**w, "x_t": stream[b]} for b in range(n)],
                               core_ids=list(range(n)))
    stream = [np.asarray(res.results[b]["y_t"]) for b in range(n)]

    nc2 = build_moe_only("l0")
    w = host_moe_inputs("l0", rd)
    res = run_bass_kernel_spmd(nc2, [{**w, "x_t": stream[b]} for b in range(n)],
                               core_ids=list(range(n)))
    stream = [np.asarray(res.results[b]["y_t"]) for b in range(n)]

    nc3 = build_l1attn_only()
    w = host_l1attn_inputs(rd)
    res = run_bass_kernel_spmd(nc3, [{**w, "x_t": stream[b]} for b in range(n)],
                               core_ids=list(range(n)))
    stream = [np.asarray(res.results[b]["y_t"]) for b in range(n)]

    nc4 = build_moe_only("l1")
    w = host_moe_inputs("l1", rd)
    res = run_bass_kernel_spmd(nc4, [{**w, "x_t": stream[b]} for b in range(n)],
                               core_ids=list(range(n)))
    stream = [np.asarray(res.results[b]["y_t"]) for b in range(n)]

    return np.stack([s.T for s in stream]).astype(np.float32)
